# version 06
# speedup vs baseline: 1.2396x; 1.0381x over previous
import math
import numpy as np
import ml_dtypes
from contextlib import ExitStack
import concourse.bass as bass
import concourse.mybir as mybir
from concourse.bass_utils import run_bass_kernel_spmd

F32 = mybir.dt.float32
BF16 = mybir.dt.bfloat16
I8 = mybir.dt.int8
AF = mybir.ActivationFunctionType
ALU = mybir.AluOpType
BF = ml_dtypes.bfloat16

P = 128
T = 1024
KC = 16
H = 16
NCORES = 8
L0_SLOPES = [2.0 ** (-(i + 1) / 2.0) for i in range(16)]


def _colblock(w):
    din, ncol = w.shape
    kcn = din // 128
    a = np.ascontiguousarray(w).reshape(kcn, 128, ncol)
    return np.ascontiguousarray(a.transpose(1, 0, 2).reshape(128, kcn * ncol))


def _tilestream(wT):
    return np.stack([_colblock(wT[:, j * 128:(j + 1) * 128]) for j in range(wT.shape[1] // 128)])


def prep_inputs(inputs):
    x = np.asarray(inputs["L0_attn_x"])
    f32 = np.float32

    wqkvT0 = np.ascontiguousarray(np.asarray(inputs["L0_attn_w_qkv"]).T)
    v0_stream = np.stack([_colblock(wqkvT0[:, 4096 + cg * 256: 4096 + (cg + 1) * 256]) for cg in range(8)])
    qk0_stream = np.stack([
        _colblock(np.concatenate([
            wqkvT0[:, h * 128:(h + 1) * 128].reshape(16, 128, 128),
            wqkvT0[:, 2048 + h * 128: 2048 + (h + 1) * 128].reshape(16, 128, 128)],
            axis=2).reshape(16 * 128, 256))
        for h in range(16)])
    wout0_stream = _tilestream(np.ascontiguousarray(np.asarray(inputs["L0_attn_w_out"]).T))

    wguT = np.ascontiguousarray(np.asarray(inputs["L0_ffn_w_gate_up"]).T)
    wpjT = np.ascontiguousarray(np.asarray(inputs["L0_ffn_w_proj"]).T)

    wqkvT1 = np.ascontiguousarray(np.asarray(inputs["L1_attn_w_qkv"]).T)
    v1cols = np.concatenate([wqkvT1[:, h * 384 + 256: h * 384 + 384] for h in range(16)], axis=1)
    v1_stream = np.stack([_colblock(v1cols[:, cg * 256:(cg + 1) * 256]) for cg in range(8)])
    qk1_stream = np.stack([_colblock(wqkvT1[:, h * 384: h * 384 + 256]) for h in range(16)])
    wgate1_stream = _tilestream(np.ascontiguousarray(np.asarray(inputs["L1_attn_w_gate"]).T))
    wout1_stream = _tilestream(np.ascontiguousarray(np.asarray(inputs["L1_attn_w_out"]).T))

    wg1 = np.asarray(inputs["L1_ffn_w_gate"])
    wu1 = np.asarray(inputs["L1_ffn_w_up"])
    gu1_stream = np.stack([
        np.concatenate([_colblock(wg1[:, j * 128:(j + 1) * 128]),
                        _colblock(wu1[:, j * 128:(j + 1) * 128])], axis=1)
        for j in range(64)])
    wd1 = np.asarray(inputs["L1_ffn_w_down"])
    wdown_lin = np.ascontiguousarray(
        wd1.reshape(8, 8, 128, 4, 512).transpose(3, 0, 2, 1, 4).reshape(4, 8, 128, 4096))

    bq = np.asarray(inputs["L0_attn_b_qkv"])
    shared = {
        "v0_stream": v0_stream.astype(f32), "qk0_stream": qk0_stream.astype(f32),
        "wout0_stream": wout0_stream.astype(f32),
        "v1_stream": v1_stream.astype(f32), "qk1_stream": qk1_stream.astype(f32),
        "wgate1_stream": wgate1_stream.astype(f32), "wout1_stream": wout1_stream.astype(f32),
        "gu1_stream": gu1_stream, "wdown_lin": wdown_lin,
        "bq_sb": np.ascontiguousarray(bq[:2048].reshape(16, 128).T).astype(f32),
        "bk_sb": np.ascontiguousarray(bq[2048:4096].reshape(16, 128).T).astype(f32),
        "bv_row": np.ascontiguousarray(bq[4096:6144][None, :]).astype(BF),
        "bo_sb": np.ascontiguousarray(np.asarray(inputs["L0_attn_b_out"]).reshape(16, 128).T).astype(f32),
        "normw_sb": np.ascontiguousarray(np.asarray(inputs["L1_attn_norm_w"]).reshape(16, 128).T).astype(f32),
        "sgate_sb": np.ascontiguousarray(np.asarray(inputs["L1_ffn_s_gate"]).reshape(64, 128).T).astype(f32),
        "sup_sb": np.ascontiguousarray(np.asarray(inputs["L1_ffn_s_up"]).reshape(64, 128).T).astype(f32),
        "sdown_sb": np.ascontiguousarray(np.asarray(inputs["L1_ffn_s_down"]).reshape(16, 128).T).astype(f32),
        "slopes_bc": np.ascontiguousarray(np.broadcast_to(np.asarray(inputs["L1_attn_slope_rate"]).reshape(1, 16), (128, 16))).astype(f32),
    }
    ident = np.eye(128, dtype=BF)
    ones128 = np.ones((128, 128), dtype=BF)
    Ahalf = np.zeros((128, 128), dtype=f32)
    for k in range(128):
        Ahalf[k, (k // 64) * 64:(k // 64) * 64 + 64] = 0.5 / 64
    ii = np.arange(128)[:, None]
    jj = np.arange(128)[None, :]
    jj2 = np.arange(256)[None, :]
    shared.update({
        "ident": ident, "ones128": ones128, "Ahalf": Ahalf.astype(BF),
        "tpos1": np.broadcast_to(np.arange(1, 1025, dtype=f32), (128, 1024)).copy(),
        "DT128": np.where(jj >= ii, (jj - ii).astype(f32), f32(1e9)).copy(),
        "DT256a": np.where(jj2 >= ii, (jj2 - ii).astype(f32), f32(1e9)).copy(),
        "DT256b": np.where(jj2 >= ii + 128, (jj2 - ii - 128).astype(f32), f32(1e9)).copy(),
        "c127": (127.0 - np.arange(128, dtype=f32))[:, None].copy(),
        "eps_sb": np.broadcast_to(np.array([1e-5, 1e-6, 1.1920929e-7], dtype=f32), (128, 3)).copy(),
        "c255": (255.0 - np.arange(128, dtype=f32))[:, None].copy(),
    })

    maps = []
    for c in range(NCORES):
        b, half = c // 2, c % 2
        m = dict(shared)
        m["xT"] = np.ascontiguousarray(x[b, half * 1024:(half + 1) * 1024, :].T).astype(f32)
        oh = np.zeros((128, 8), dtype=f32)
        if half == 1:
            oh[:, c - 1] = 1.0
        m["onehot"] = oh
        gu_sl = np.empty((8, 128, 4096), dtype=f32)
        for i in range(8):
            j = 8 * c + i
            gu_sl[i, :, :2048] = _colblock(wguT[:, j * 128:(j + 1) * 128])
            gu_sl[i, :, 2048:] = _colblock(wguT[:, 8192 + j * 128: 8192 + (j + 1) * 128])
        m["gu_slab"] = gu_sl
        m["pj_slab"] = np.ascontiguousarray(
            wpjT[8 * c * 128:(8 * c + 8) * 128, :]).reshape(8, 128, 2048).astype(f32)
        maps.append(m)
    return maps



class KB:
    def __init__(self, nc, es):
        self.nc, self.es, self.sems = nc, es, {}
        self.dma_tags = set()

    def dinc(self, inst, tag):
        self.dma_tags.add(tag)
        return (tag, self.inc(inst, tag, 16))

    def s(self, name):
        if name not in self.sems:
            self.sems[name] = [self.es.enter_context(self.nc.semaphore(name)), 0]
        return self.sems[name]

    def inc(self, inst, name, amt=1):
        rec = self.s(name)
        inst.then_inc(rec[0], amt)
        rec[1] += amt
        return rec[1]

    def val(self, name):
        return self.s(name)[1]

    def wait(self, eng, name, v=None):
        rec = self.s(name)
        eng.wait_ge(rec[0], rec[1] if v is None else v)



def build_program(layers=(0, 1), taps=(), race=True):
    nc = bass.Bass("TRN2", target_bir_lowering=False, debug=False,
                   detect_race_conditions=race)
    taps = set(taps)

    def din(name, shape, dt):
        return nc.dram_tensor(name, list(shape), dt, kind="ExternalInput").ap()

    def dram_out(name, shape, dt):
        return nc.dram_tensor(name, list(shape), dt, kind="ExternalOutput").ap()

    def dint(name, shape, dt, **kw):
        return nc.dram_tensor(name, list(shape), dt, kind="Internal", **kw).ap()

    D = {}
    D["xT"] = din("xT", [2048, 1024], F32)
    if 0 not in layers:
        D["x1T"] = din("x1T", [2048, 1024], F32)
    for n, s, dt in [("v0_stream", (8, 128, 4096), F32), ("qk0_stream", (16, 128, 4096), F32),
                     ("wout0_stream", (16, 128, 2048), F32), ("v1_stream", (8, 128, 4096), F32),
                     ("qk1_stream", (16, 128, 4096), F32), ("wgate1_stream", (16, 128, 2048), F32),
                     ("wout1_stream", (16, 128, 2048), F32), ("gu1_stream", (64, 128, 4096), I8),
                     ("wdown_lin", (4, 8, 128, 4096), I8), ("gu_slab", (8, 128, 4096), F32),
                     ("pj_slab", (8, 128, 2048), F32), ("bq_sb", (128, 16), F32),
                     ("bk_sb", (128, 16), F32), ("bv_row", (1, 2048), BF16),
                     ("bo_sb", (128, 16), F32), ("normw_sb", (128, 16), F32),
                     ("sgate_sb", (128, 64), F32), ("sup_sb", (128, 64), F32),
                     ("sdown_sb", (128, 16), F32), ("slopes_bc", (128, 16), F32),
                     ("ident", (128, 128), BF16), ("ones128", (128, 128), BF16),
                     ("Ahalf", (128, 128), BF16), ("tpos1", (128, 1024), F32),
                     ("DT128", (128, 128), F32), ("DT256a", (128, 256), F32),
                     ("DT256b", (128, 256), F32), ("c127", (128, 1), F32),
                     ("c255", (128, 1), F32), ("onehot", (128, 8), F32),
                     ("eps_sb", (128, 3), F32)]:
        D[n] = din(n, s, dt)
    D["out"] = dram_out("out", [2048, 1024], F32)

    D["x_spill"] = dint("x_spill", [2048, 1024], F32)
    D["qspill"] = dint("qspill", [2048, 1024], BF16)
    D["h0"] = dint("h0", [2, 8, 128, 4096], BF16)
    D["gu_in"] = dint("gu_in", [1024, 4096], BF16)
    D["gu_out"] = dint("gu_out", [8192, 4096], BF16, addr_space="Shared")
    D["pj_in"] = dint("pj_in", [1024, 2048], BF16)
    D["pj_out"] = dint("pj_out", [8192, 2048], BF16, addr_space="Shared")
    D["wpj_lin"] = dint("wpj_lin", [4, 8, 128, 4096], BF16)
    D["st_in"] = [dint(f"st_in{L}", [128, 2048], F32) for L in range(2)]
    D["st_out"] = [dint(f"st_out{L}", [1024, 2048], F32, addr_space="Shared") for L in range(2)]

    es = ExitStack()
    kb = KB(nc, es)
    sb = lambda n, s, dt: es.enter_context(nc.sbuf_tensor("sb_" + n, list(s), dt))
    pt = lambda n, s, dt: es.enter_context(nc.psum_tensor(n, list(s), dt))

    S = {}
    for n, s, dt in [("ident", (128, 128), BF16), ("ones", (128, 128), BF16),
                     ("Ahalf", (128, 128), BF16), ("tpos1", (128, 1024), F32),
                     ("DT128", (128, 128), F32), ("DT256a", (128, 256), F32),
                     ("DT256b", (128, 256), F32), ("c127", (128, 1), F32),
                     ("c255", (128, 1), F32), ("onehot", (128, 8), F32),
                     ("epsv", (128, 3), F32),
                     ("bq", (128, 16), F32), ("bk", (128, 16), F32),
                     ("bv", (1, 2048), BF16), ("bo", (128, 16), F32),
                     ("normw", (128, 16), F32), ("sgate", (128, 64), F32),
                     ("sup", (128, 64), F32), ("sdown", (128, 16), F32), ("scols", (128, 16), F32),
                     ("R1", (128, KC * T), BF16), ("R2", (128, KC * T), BF16),
                     ("R3", (128, KC * T), BF16),
                     ("qh", (128, T), BF16), ("kh", (128, T), BF16),
                     ("Ssb", (128, 2048), F32), ("Sprev", (128, 2048), F32),
                     ("mean", (128, T), F32), ("rstd", (128, T), F32),
                     ("rstdf", (128, T), F32), ("fscr", (128, 4096), F32),
                     ("sq", (128, T), BF16), ("hj0", (128, T), BF16),
                     ("hj1", (128, T), BF16), ("mska", (128, 256), BF16),
                     ("mskb", (128, 256), BF16), ("qdec", (128, 256), F32),
                     ("kdT", (128, 256), BF16), ("qd", (128, 256), F32),
                     ("kcol", (128, 2), F32), ("bd", (128, 1), F32),
                     ("wring", (128, 12288), BF16), ("qscr", (128, 8192), BF16)]:
        S[n] = sb(n, s, dt)
    PS = {}
    for i in range(4):
        PS[f"mm{i}"] = pt(f"pmm{i}", (128, 512), F32).ap()
    PS["ex0"] = pt("pex0", (128, 512), F32).ap()
    PS["ex1"] = pt("pex1", (128, 512), F32).ap()
    _pscan = pt("pscan", (128, 512), F32).ap()
    PS["sca"] = _pscan[:, 0:256]
    PS["scb"] = _pscan[:, 256:512]
    PS["scanfull"] = _pscan
    _pb7 = pt("pb7", (128, 512), F32).ap()
    PS["b7full"] = _pb7
    PS["o"] = _pb7[:, 0:256]
    PS["kv"] = _pb7[:, 256:384]
    PS["tp"] = _pb7[:, 384:448].bitcast(BF16)

    ctx = dict(nc=nc, kb=kb, D=D, S=S, PS=PS, taps=taps, tapio={},
               dram_out=dram_out, marks={}, layers=layers)

    with es:
        block = es.enter_context(nc.Block())
        ctx["block"] = block

        @block.sync
        def _(q):
            names = ["ident", "ones", "Ahalf", "tpos1", "DT128", "DT256a", "DT256b",
                     "c127", "c255", "onehot", "bq", "bk", "bv", "bo", "normw",
                     "sgate", "sup", "sdown", "scols", "epsv"]
            src = ["ident", "ones128", "Ahalf", "tpos1", "DT128", "DT256a", "DT256b",
                   "c127", "c255", "onehot", "bq_sb", "bk_sb", "bv_row", "bo_sb",
                   "normw_sb", "sgate_sb", "sup_sb", "sdown_sb", "slopes_bc", "eps_sb"]
            for dst, sr in zip(names, src):
                i = q.dma_start(out=S[dst][:], in_=D[sr])
                kb.dinc(i, "cst")
        ctx["marks"]["consts"] = ("cst", kb.val("cst"))


        if 0 in layers:
            with nc.named_scope("quant"):
                emit_ln_loads(ctx, 0)
                emit_quant(ctx, ln_hooks=make_ln_stats(ctx))
        for L in layers:
            with nc.named_scope(f"ln{L}"):
                if L == 0 and 0 in layers:
                    emit_ln_finish(ctx, 0)
                else:
                    emit_ln_loads(ctx, L)
                    emit_ln_standalone(ctx)
                    emit_ln_finish(ctx, L)
            with nc.named_scope(f"v{L}"):
                emit_v(ctx, L)
            with nc.named_scope(f"scan{L}"):
                emit_qk_scan(ctx, L)
            if L == 1:
                with nc.named_scope("gate1"):
                    emit_gate(ctx)
            with nc.named_scope(f"ffnA{L}"):
                emit_ffnA(ctx, L)
            if L == 0:
                with nc.named_scope("ffnB0"):
                    emit_ffnB(ctx, L)
                with nc.named_scope("corr0"):
                    emit_corr(ctx, L)
            else:
                with nc.named_scope("corr1"):
                    emit_corr(ctx, L)
                with nc.named_scope("normz"):
                    emit_normz(ctx)
                with nc.named_scope("ffnB1"):
                    emit_ffnB(ctx, L)
            with nc.named_scope(f"out{L}"):
                emit_out(ctx, L)

        @block.sync
        def _(q):
            for tag in sorted(kb.dma_tags):
                kb.wait(q, tag)

    return nc, ctx


def emit_tap(ctx, name, src, shape, dt, wait=()):
    if name not in ctx["taps"]:
        return
    kb, block = ctx["kb"], ctx["block"]
    t = ctx["dram_out"]("tap_" + name, list(shape), dt)
    ctx["tapio"][name] = t

    @block.sync
    def _(q):
        for tag, v in wait:
            kb.wait(q, tag, v)
        i = q.dma_start(out=t, in_=src)
        kb.dinc(i, "tapq")



def emit_quant(ctx, ln_hooks=None):
    kb, block, D, S, PS = ctx["kb"], ctx["block"], ctx["D"], ctx["S"], ctx["PS"]
    units = []
    for i in range(8):
        units.append((D["gu_slab"][i, :, 0:2048], D["gu_in"][i * 128:(i + 1) * 128, 0:2048]))
        units.append((D["gu_slab"][i, :, 2048:4096], D["gu_in"][i * 128:(i + 1) * 128, 2048:4096]))
    for i in range(8):
        units.append((D["pj_slab"][i], D["pj_in"][i * 128:(i + 1) * 128, :]))
    nun = len(units)
    slots = [S["wring"][:, 0:2048], S["wring"][:, 2048:4096]]
    a_t = S["qscr"][:, 0:2048]
    sg_t = S["qscr"][:, 2048:4096]
    ind_t = S["qscr"][:, 4096:6144]
    tb_t = S["qscr"][:, 6144:8192]
    @block.gpsimd
    def _(g):
        for u, (src, _) in enumerate(units):
            if u >= 2:
                kb.wait(g, f"quS{u % 2}", 16 * (u // 2))
            i = g.dma_start(out=slots[u % 2][:], in_=src)
            kb.dinc(i, f"quL{u % 2}")

    cm = ctx["marks"]["consts"]

    @block.scalar
    def _(s):
        for u in range(nun):
            if u == 0:
                kb.wait(s, cm[0], cm[1])
            kb.wait(s, f"quL{u % 2}", 16 * (u // 2 + 1))
            kb.wait(s, "qdve", u)
            i = s.activation(out=a_t[:], in_=slots[u % 2][:], func=AF.Abs)
            kb.inc(i, "qa", 1)
            i = s.activation(out=sg_t[:], in_=slots[u % 2][:], func=AF.Sign)
            kb.inc(i, "qa", 1)
            if ln_hooks and u < KC:
                ln_hooks[0](s, u)

    @block.tensor
    def _(t):
        for u in range(nun):
            kb.wait(t, "qa", 2 * u + 1)
            kb.wait(t, "qdve", u)
            for n in range(4):
                mm = t.matmul(PS[f"mm{n}"][:], S["Ahalf"][:],
                              a_t[:, n * 512:(n + 1) * 512], start=True, stop=True)
            kb.inc(mm, "qpe", 1)
            if ln_hooks and u < KC:
                ln_hooks[1](t, u)

    @block.vector
    def _(v):
        for u in range(nun):
            kb.wait(v, "qpe", u + 1)
            kb.wait(v, "qa", 2 * u + 2)
            for n in range(4):
                i = v.tensor_scalar(tb_t[:, n * 512:(n + 1) * 512], PS[f"mm{n}"][:],
                                    0.5e-8, None, ALU.max)
            v.tensor_tensor(ind_t[:], a_t[:], tb_t[:], ALU.is_gt)
            v.tensor_tensor(a_t[:], ind_t[:], sg_t[:], ALU.mult)
            i = v.scalar_tensor_tensor(slots[u % 2][:], a_t[:], 2.0, tb_t[:],
                                       ALU.mult, ALU.mult)
            kb.inc(i, "qdve", 1)

    @block.sync
    def _(q):
        for u, (_, dst) in enumerate(units):
            kb.wait(q, "qdve", u + 1)
            i = q.dma_start(out=dst, in_=slots[u % 2][:])
            kb.dinc(i, f"quS{u % 2}")
    ctx["marks"]["quant_store"] = [("quS0", kb.val("quS0")), ("quS1", kb.val("quS1"))]

    @block.gpsimd
    def _(g):
        for tg, vv in ctx["marks"]["quant_store"]:
            kb.wait(g, tg, vv)
        i = g.collective_compute("AllGather", ALU.bypass,
                                 replica_groups=[list(range(NCORES))],
                                 ins=[D["gu_in"].opt()], outs=[D["gu_out"].opt()])
        ctx["marks"]["cc_gu"] = kb.inc(i, "ccq", 1)
        i = g.collective_compute("AllGather", ALU.bypass,
                                 replica_groups=[list(range(NCORES))],
                                 ins=[D["pj_in"].opt()], outs=[D["pj_out"].opt()])
        ctx["marks"]["cc_pj"] = kb.inc(i, "ccq", 1)
    ctx["marks"]["wring_fence"] = [("quS0", kb.val("quS0")), ("quS1", kb.val("quS1"))]
    emit_tap(ctx, "guhat", D["gu_out"], (8192, 4096), BF16, [("ccq", ctx["marks"]["cc_gu"])])
    emit_tap(ctx, "pjhat", D["pj_out"], (8192, 2048), BF16, [("ccq", ctx["marks"]["cc_pj"])])



def emit_ln_loads(ctx, L):
    kb, block, D, S = ctx["kb"], ctx["block"], ctx["D"], ctx["S"]
    if L == 0:
        x_src = D["xT"]
    else:
        x_src = D["x_spill"] if 0 in ctx["layers"] else D["x1T"]
    xf = [S["R2"][:].bitcast(F32), S["R3"][:].bitcast(F32)]
    x_ready = ctx["marks"].get("x1mark") if L == 1 else None
    base = [kb.val("lnx0"), kb.val("lnx1")]

    @block.scalar
    def _(q):
        if x_ready:
            for tg, vv in x_ready:
                kb.wait(q, tg, vv)
        for kc in range(KC):
            i = q.dma_start(out=xf[kc // 8][:, (kc % 8) * 1024:(kc % 8 + 1) * 1024],
                            in_=x_src[kc * 128:(kc + 1) * 128, :])
            kb.dinc(i, f"lnx{kc % 2}")
    ctx["lnstate"] = {"xf": xf, "lnx": base,
                      "B": {t: kb.val("ln" + t) for t in
                            ["squ", "sqa", "st", "var", "sqrt", "rs", "xn"]}}


def make_ln_stats(ctx):
    kb, S, PS = ctx["kb"], ctx["S"], ctx["PS"]
    ph = "ln"
    st = ctx["lnstate"]
    B, xf, lnx = st["B"], st["xf"], st["lnx"]
    r1f = ctx["marks"].get("R1_fence")
    sqf = ("fAsqu", kb.val("fAsqu"))
    bank_waits = [("sdve", kb.val("sdve")), ("fBv", kb.val("fBv")),
                  ("nzv", kb.val("nzv")), ("fAsil", kb.val("fAsil")),
                  ("fBgt", kb.val("fBgt")), ("fBgact", kb.val("fBgact"))]
    R1 = S["R1"]
    SUM = [PS["ex0"], PS["ex1"]]
    SQ = [PS["scanfull"], PS["b7full"]]

    def act_hook(s, kc):
        xt = xf[kc // 8][:, (kc % 8) * 1024:(kc % 8 + 1) * 1024]
        kb.wait(s, f"lnx{kc % 2}", lnx[kc % 2] + 16 * (kc // 2 + 1))
        if kc == 0:
            if r1f:
                kb.wait(s, r1f[0], r1f[1])
            if sqf[1]:
                kb.wait(s, sqf[0], sqf[1])
        kb.wait(s, ph + "squ", B["squ"] + kc)
        s.activation(out=R1[:, kc * T:(kc + 1) * T], in_=xt, func=AF.Identity)
        i = s.activation(out=S["sq"][:], in_=xt, func=AF.Square)
        kb.inc(i, ph + "sqa", 1)

    def pe_hook(t, kc):
        kb.wait(t, ph + "sqa", B["sqa"] + kc + 1)
        if kc == 0:
            for tag, v_ in bank_waits:
                if v_:
                    kb.wait(t, tag, v_)
        for hf in range(2):
            t.matmul(SUM[hf], S["ones"][:],
                     R1[:, kc * T + hf * 512: kc * T + hf * 512 + 512],
                     start=(kc == 0), stop=(kc == KC - 1))
            mm = t.matmul(SQ[hf], S["ones"][:],
                          S["sq"][:, hf * 512:(hf + 1) * 512],
                          start=(kc == 0), stop=(kc == KC - 1))
        kb.inc(mm, ph + "squ", 1)

    return act_hook, pe_hook


def emit_ln_standalone(ctx):
    block = ctx["block"]
    act_hook, pe_hook = make_ln_stats(ctx)
    for kc in range(KC):
        @block.scalar
        def _(s, kc=kc):
            act_hook(s, kc)

        @block.tensor
        def _(t, kc=kc):
            pe_hook(t, kc)


def emit_ln_finish(ctx, L):
    kb, block, S, PS = ctx["kb"], ctx["block"], ctx["S"], ctx["PS"]
    ph = "ln"
    st = ctx["lnstate"]
    B, xf = st["B"], st["xf"]
    R1 = S["R1"]
    t1 = S["fscr"][:, 2048:2560]
    t2 = S["fscr"][:, 2560:3072]
    SUM = [PS["ex0"], PS["ex1"]]
    SQ = [PS["scanfull"], PS["b7full"]]
    fence = ctx["marks"].get("fscr_fence")

    @block.vector
    def _(v):
        kb.wait(v, ph + "squ", B["squ"] + KC)
        if fence:
            kb.wait(v, fence[0], fence[1])
        for hf in range(2):
            sl = slice(hf * 512, (hf + 1) * 512)
            v.tensor_scalar(S["mean"][:, sl], SUM[hf], 1.0 / 2048, None, ALU.mult)
            v.tensor_scalar(t1, SQ[hf], 1.0 / 2048, None, ALU.mult)
            v.tensor_tensor(t2, S["mean"][:, sl], S["mean"][:, sl], ALU.mult)
            i = v.tensor_tensor(t1, t1, t2, ALU.subtract)
            kb.inc(i, ph + "var", 1)
            kb.wait(v, ph + "sqrt", B["sqrt"] + hf + 1)
            i = v.reciprocal(S["rstd"][:, sl], t2)
            kb.inc(i, ph + "rs", 1)

    @block.scalar
    def _(s):
        for hf in range(2):
            kb.wait(s, ph + "var", B["var"] + hf + 1)
            i = s.activation(out=t2, in_=t1, func=AF.Sqrt, bias=S['epsv'][:, 0:1])
            kb.inc(i, ph + "sqrt", 1)

    @block.vector
    def _(v):
        kb.wait(v, ph + "rs", B["rs"] + 2)
        for kc in range(KC):
            xt = xf[kc // 8][:, (kc % 8) * 1024:(kc % 8 + 1) * 1024]
            v.tensor_tensor(S["fscr"][:, 2048:3072], xt, S["mean"][:], ALU.subtract)
            i = v.tensor_tensor(R1[:, kc * T:(kc + 1) * T], S["fscr"][:, 2048:3072],
                                S["rstd"][:], ALU.mult)
            kb.inc(i, ph + "xn", 1)
    ctx["marks"]["xn_done"] = (ph + "xn", kb.val(ph + "xn"))
    emit_tap(ctx, f"xn{L}", R1[:], (128, KC * T), BF16, [ctx["marks"]["xn_done"]])



def emit_v(ctx, L):
    kb, block, D, S, PS = ctx["kb"], ctx["block"], ctx["D"], ctx["S"], ctx["PS"]
    ph = "v"
    B = {t: kb.val(ph + t) for t in ["mm", "ev"]}
    stream = D["v0_stream"] if L == 0 else D["v1_stream"]
    slots = [S["wring"][:, 0:4096], S["wring"][:, 4096:8192]]
    wfence = ctx["marks"].get("wring_fence")
    R1, R2 = S["R1"], S["R2"]

    @block.gpsimd
    def _(g):
        if wfence:
            for tg, vv in wfence:
                kb.wait(g, tg, vv)
        for cg in range(8):
            if cg >= 2:
                kb.wait(g, ph + "mm", B["mm"] + 8 * (cg - 1))
            i = g.dma_start(out=slots[cg % 2][:], in_=stream[cg])
            kb.dinc(i, f"vw{cg % 2}")
    vw0 = [kb.val("vw0") - 16 * 4, kb.val("vw1") - 16 * 4]

    @block.tensor
    def _(t):
        xnm = ctx["marks"]["xn_done"]
        kb.wait(t, xnm[0], xnm[1])
        kb.wait(t, "qdve", kb.val("qdve"))
        for cg in range(8):
            kb.wait(t, f"vw{cg % 2}", vw0[cg % 2] + 16 * (cg // 2 + 1))
            for m in range(8):
                n_ = cg * 8 + m
                bank = n_ % 4
                kb.wait(t, ph + "ev", B["ev"] + max(0, n_ - 3))
                mm = None
                for kc in range(KC):
                    mm = t.matmul(PS[f"mm{bank}"][:, 0:256],
                                  R1[:, kc * T + m * 128: kc * T + (m + 1) * 128],
                                  slots[cg % 2][:, kc * 256:(kc + 1) * 256],
                                  start=(kc == 0), stop=(kc == KC - 1 and L == 1))
                if L == 0:
                    mm = t.matmul(PS[f"mm{bank}"][:, 0:256], S["ones"][0:1, 0:128],
                                  S["bv"][0:1, cg * 256:(cg + 1) * 256],
                                  start=False, stop=True)
                kb.inc(mm, ph + "mm", 1)

    if L == 0:
        @block.vector
        def _(v):
            for cg in range(8):
                for m in range(8):
                    n = cg * 8 + m
                    kb.wait(v, ph + "mm", B["mm"] + n + 1)
                    i = v.tensor_copy(R2[:, m * 2048 + cg * 256: m * 2048 + (cg + 1) * 256],
                                      PS[f"mm{n % 4}"][:, 0:256])
                    kb.inc(i, ph + "ev", 1)
    else:
        r2f = ctx["marks"].get("R2_fence")

        @block.scalar
        def _(sc):
            if r2f:
                kb.wait(sc, r2f[0], r2f[1])
            for cg in range(8):
                for m in range(8):
                    n = cg * 8 + m
                    kb.wait(sc, ph + "mm", B["mm"] + n + 1)
                    i = sc.activation(out=R2[:, m * 2048 + cg * 256: m * 2048 + (cg + 1) * 256],
                                      in_=PS[f"mm{n % 4}"][:, 0:256], func=AF.Silu)
                    kb.inc(i, ph + "ev", 1)
    ctx["marks"]["wring_fence"] = [(ph + "mm", kb.val(ph + "mm"))]
    ctx["marks"]["v_done"] = (ph + "ev", kb.val(ph + "ev"))
    emit_tap(ctx, f"vtok{L}", R2[:], (128, KC * T), BF16, [ctx["marks"]["v_done"]])



def emit_qk_scan(ctx, L):
    kb, block, D, S, PS = ctx["kb"], ctx["block"], ctx["D"], ctx["S"], ctx["PS"]
    ph = "s"
    B = {t: kb.val(ph + t) for t in ["init", "qk", "act", "pe", "dve", "msk"]}
    stream = D["qk0_stream"] if L == 0 else D["qk1_stream"]
    slots = [S["wring"][:, 0:4096], S["wring"][:, 4096:8192]]
    wfence = ctx["marks"]["wring_fence"]
    R1, R2, R3 = S["R1"], S["R2"], S["R3"]
    qh, khh = S["qh"], S["kh"]
    nch = 8 if L == 0 else 4
    CH = 128 if L == 0 else 256
    sfence = ctx["marks"].get("st_dma0") if L == 1 else None
    vdm = ctx["marks"]["v_done"]

    @block.vector
    def _(v):
        kb.wait(v, vdm[0], vdm[1])
        if sfence:
            kb.wait(v, sfence[0], sfence[1])
        i = v.memset(S["Ssb"][:], 0.0)
        kb.inc(i, ph + "init", 1)

    @block.gpsimd
    def _(g):
        for tg, vv in wfence:
            kb.wait(g, tg, vv)
        for h in range(H):
            if h >= 2:
                kb.wait(g, ph + "qk", B["qk"] + h - 1)
            i = g.dma_start(out=slots[h % 2][:], in_=stream[h])
            kb.dinc(i, f"qkw{h % 2}")
        if L == 0 and "cc_pj" in ctx["marks"]:
            kb.wait(g, "ccq", ctx["marks"]["cc_pj"])
            for mg in range(4):
                src4 = D["pj_out"][:, mg * 512:(mg + 1) * 512].rearrange(
                    "(blk jj p) c -> jj blk p c", blk=8, jj=8, p=128)
                for jj in range(8):
                    i = g.dma_start(out=D["wpj_lin"][mg][:, :, jj * 512:(jj + 1) * 512],
                                    in_=src4[jj])
                    kb.dinc(i, "rtl")
            ctx["marks"]["retile"] = ("rtl", kb.val("rtl"))
    qkw0 = [kb.val("qkw0") - 16 * 8, kb.val("qkw1") - 16 * 8]

    qsp_marks = []
    mskmarks = {}
    actbases = {}
    headaux = {}
    kpp = KC // nch
    kdTf = S["qscr"][:, 0:1024]
    exA = PS["ex0"].bitcast(BF16)

    def qk_piece(h, pc):
        slot = slots[h % 2]

        @block.tensor
        def _(t):
            if pc == 0:
                if h == 0:
                    kb.wait(t, ph + "init", B["init"] + 1)
                kb.wait(t, f"qkw{h % 2}", qkw0[h % 2] + 16 * (h // 2 + 1))
                if h >= 1:
                    kb.wait(t, ph + "act", actbases[h])
            mmk = None
            for kc in range(pc * kpp, (pc + 1) * kpp):
                for hf in range(2):
                    t.matmul(PS[f"mm{hf}"][:],
                             slot[:, kc * 256: kc * 256 + 128],
                             R1[:, kc * T + hf * 512: kc * T + hf * 512 + 512],
                             start=(kc == 0), stop=(kc == KC - 1))
                for hf in range(2):
                    mmk = t.matmul(PS[f"mm{2 + hf}"][:],
                                   slot[:, kc * 256 + 128: kc * 256 + 256],
                                   R1[:, kc * T + hf * 512: kc * T + hf * 512 + 512],
                                   start=(kc == 0), stop=(kc == KC - 1))
            if pc == nch - 1:
                kb.inc(mmk, ph + "qk", 1)

    def head_prep(h):
        amark = kb.val(ph + "act")

        @block.tensor
        def _(t):
            kb.wait(t, ph + "act", amark)
            if h == 0:
                kb.wait(t, "lnrs", kb.val("lnrs"))
            mm = None
            if L == 0:
                for c in range(nch):
                    mm = t.transpose(exA[:, c * 128:(c + 1) * 128],
                                     khh[:, c * 128:(c + 1) * 128], S["ident"][:])
            else:
                for c in range(nch):
                    t.transpose(exA[:, c * 128:(c + 1) * 128],
                                khh[:, c * 256: c * 256 + 128], S["ident"][:])
                    mm = t.transpose(exA[:, 512 + c * 128: 512 + (c + 1) * 128],
                                     khh[:, c * 256 + 128: c * 256 + 256], S["ident"][:])
            kb.inc(mm, ph + "pe", 1)
        pmark = kb.val(ph + "pe")

        @block.vector
        def _(v):
            kb.wait(v, ph + "pe", pmark)
            kb.wait(v, ph + "msk", mskmarks[h])
            if L == 1 and h == 0:
                kb.wait(v, "fBmm", kb.val("fBmm"))
            if L == 0:
                i = v.tensor_scalar(kdTf[:], exA[:], S["kcol"][:, 0:1], None, ALU.mult)
                kb.inc(i, ph + "dve", 1)
            else:
                v.tensor_scalar(kdTf[:, 0:512], exA[:, 0:512], S["kcol"][:, 0:1],
                                None, ALU.mult)
                i = v.tensor_scalar(kdTf[:, 512:1024], exA[:, 512:1024],
                                    S["kcol"][:, 1:2], None, ALU.mult)
                kb.inc(i, ph + "dve", 2)
        headaux[h] = {"postscale": kb.val(ph + "dve"), "stt": {}}

    def scan_chunk(h, c, piece_fn=None):
        slope = L0_SLOPES[h] if L == 0 else None
        lam_c = math.exp(-128.0 * slope) if L == 0 else None
        sl = slice(c * CH, (c + 1) * CH)
        SC = PS["scanfull"] if c % 2 == 0 else PS["b7full"]
        OB = PS["ex0"] if c % 2 == 0 else PS["ex1"]
        Sh = S["Ssb"][:, h * 128:(h + 1) * 128]
        if L == 0:
            vt0 = R2[:, c * 2048 + h * 128: c * 2048 + (h + 1) * 128]
            o_ap = OB[:, 0:128]
            kv_ap = OB[:, 128:256]
        else:
            vt0 = R2[:, (2 * c) * 2048 + h * 128: (2 * c) * 2048 + (h + 1) * 128]
            vt1 = R2[:, (2 * c + 1) * 2048 + h * 128: (2 * c + 1) * 2048 + (h + 1) * 128]
            o_ap = OB[:, 0:256]
            kv_ap = OB[:, 256:384]

        @block.tensor
        def _(t):
            if c == 0:
                kb.wait(t, ph + "msk", mskmarks[h])
                kb.wait(t, ph + "dve", headaux[h]["postscale"])
            else:
                kb.wait(t, ph + "dve", headaux[h]["stt"][c - 1])
            if L == 0:
                t.matmul(SC[:, 0:128], khh[:, sl], qh[:, sl], start=True, stop=True)
                i = t.matmul(kv_ap, kdTf[:, c * 128:(c + 1) * 128], vt0,
                             start=True, stop=True)
            else:
                t.matmul(SC[:, 0:256], khh[:, c * 256: c * 256 + 128], qh[:, sl],
                         start=True, stop=True)
                t.matmul(SC[:, 256:512], khh[:, c * 256 + 128: c * 256 + 256],
                         qh[:, sl], start=True, stop=True)
                t.matmul(kv_ap, kdTf[:, c * 128:(c + 1) * 128], vt0,
                         start=True, stop=False)
                i = t.matmul(kv_ap, kdTf[:, 512 + c * 128: 512 + (c + 1) * 128], vt1,
                             start=False, stop=True)
            kb.inc(i, ph + "pe", 1)
        spe = kb.val(ph + "pe")

        @block.vector
        def _(v):
            kb.wait(v, ph + "pe", spe)
            if L == 0:
                v.tensor_tensor(S["sq"][:, 0:128], SC[:, 0:128], S["mska"][:, 0:128],
                                ALU.mult)
                i = v.tensor_tensor(S["qd"][:, 0:128], qh[:, sl], S["qdec"][:, 0:128],
                                    ALU.mult)
                kb.inc(i, ph + "dve", 2)
            else:
                v.tensor_tensor(S["sq"][:, 0:256], SC[:, 0:256], S["mska"][:], ALU.mult)
                v.tensor_tensor(S["sq"][:, 256:512], SC[:, 256:512], S["mskb"][:],
                                ALU.mult)
                i = v.tensor_tensor(S["qd"][:], qh[:, sl], S["qdec"][:], ALU.mult)
                kb.inc(i, ph + "dve", 3)
        preo = kb.val(ph + "dve")

        if piece_fn:
            piece_fn()

        @block.tensor
        def _(t):
            kb.wait(t, ph + "dve", preo)
            if L == 0:
                t.matmul(o_ap, vt0, S["sq"][:, 0:128], start=True, stop=False)
                i = t.matmul(o_ap, Sh, S["qd"][:, 0:128], start=False, stop=True)
            else:
                t.matmul(o_ap, vt0, S["sq"][:, 0:256], start=True, stop=False)
                t.matmul(o_ap, vt1, S["sq"][:, 256:512], start=False, stop=False)
                i = t.matmul(o_ap, Sh, S["qd"][:], start=False, stop=True)
            kb.inc(i, ph + "pe", 1)
        ope = kb.val(ph + "pe")

        @block.vector
        def _(v):
            kb.wait(v, ph + "pe", ope)
            W = CH
            i = v.tensor_copy(R3[:, h * T + c * W: h * T + (c + 1) * W], o_ap[:, 0:W])
            kb.inc(i, ph + "dve", 1)
            if L == 0:
                i = v.scalar_tensor_tensor(Sh, Sh, float(lam_c), kv_ap,
                                           ALU.mult, ALU.add)
            else:
                i = v.scalar_tensor_tensor(Sh, Sh, S["bd"][:, 0:1], kv_ap,
                                           ALU.mult, ALU.add)
            kb.inc(i, ph + "dve", 1)
        headaux[h]["stt"][c] = kb.val(ph + "dve")

    for h in range(H):
        slope = L0_SLOPES[h] if L == 0 else None
        scol = S["scols"][:, h:h + 1]
        actbases[h] = kb.val(ph + "act")
        preB = kb.val(ph + "pre")

        if h == 0:
            for c in range(nch):
                qk_piece(0, c)
        else:
            for c in range(nch):
                scan_chunk(h - 1, c, piece_fn=lambda c=c: qk_piece(h, c))
        qkmark = kb.val(ph + "qk")
        pbase = kb.val(ph + "pe")
        dbase = kb.val(ph + "dve")

        if L == 1:
            @block.vector
            def _(v):
                if h == 0:
                    cm = ctx["marks"]["consts"]
                    kb.wait(v, cm[0], cm[1])
                kb.wait(v, ph + "msk", B["msk"] + h)
                v.tensor_scalar(S["fscr"][:, 0:256], S["DT256a"][:], scol, None, ALU.mult)
                v.tensor_scalar(S["fscr"][:, 256:512], S["DT256b"][:], scol, None, ALU.mult)
                v.tensor_scalar(S["fscr"][:, 512:768], S["tpos1"][:, 0:256], scol, None, ALU.mult)
                v.tensor_scalar(S["fscr"][:, 768:769], S["c255"][:], scol, None, ALU.mult)
                i = v.tensor_scalar(S["fscr"][:, 769:770], S["c127"][:], scol, None, ALU.mult)
                kb.inc(i, ph + "pre", 5)

        @block.scalar
        def _(s):
            kb.wait(s, ph + "qk", qkmark)
            if h >= 1:
                kb.wait(s, qsp_marks[-1][0], qsp_marks[-1][1])
                kb.wait(s, ph + "pe", pbase)
                kb.wait(s, ph + "dve", dbase)
            for hf in range(2):
                if L == 0:
                    s.activation(out=qh[:, hf * 512:(hf + 1) * 512], in_=PS[f"mm{hf}"][:],
                                 func=AF.Identity, bias=S["bq"][:, h:h + 1])
                else:
                    s.activation(out=qh[:, hf * 512:(hf + 1) * 512], in_=PS[f"mm{hf}"][:],
                                 func=AF.Silu)
            for hf in range(2):
                if L == 0:
                    i = s.activation(out=khh[:, hf * 512:(hf + 1) * 512], in_=PS[f"mm{2 + hf}"][:],
                                     func=AF.Identity, bias=S["bk"][:, h:h + 1])
                else:
                    i = s.activation(out=khh[:, hf * 512:(hf + 1) * 512], in_=PS[f"mm{2 + hf}"][:],
                                     func=AF.Silu)
                kb.inc(i, ph + "act", 1)
            if L == 0:
                s.activation(out=S["mska"][:, 0:128], in_=S["DT128"][:], func=AF.Exp,
                             scale=-slope)
                s.activation(out=S["qdec"][:, 0:128], in_=S["tpos1"][:, 0:128],
                             func=AF.Exp, scale=-slope)
                i = s.activation(out=S["kcol"][:, 0:1], in_=S["c127"][:], func=AF.Exp,
                                 scale=-slope)
                kb.inc(i, ph + "msk", 1)
            else:
                kb.wait(s, ph + "pre", preB + 5)
                s.activation(out=S["mska"][:], in_=S["fscr"][:, 0:256], func=AF.Exp,
                             scale=-1.0)
                s.activation(out=S["mskb"][:], in_=S["fscr"][:, 256:512], func=AF.Exp,
                             scale=-1.0)
                s.activation(out=S["qdec"][:], in_=S["fscr"][:, 512:768], func=AF.Exp,
                             scale=-1.0)
                s.activation(out=S["kcol"][:, 0:1], in_=S["fscr"][:, 768:769], func=AF.Exp,
                             scale=-1.0)
                s.activation(out=S["kcol"][:, 1:2], in_=S["fscr"][:, 769:770], func=AF.Exp,
                             scale=-1.0)
                i = s.activation(out=S["bd"][:], in_=scol, func=AF.Exp, scale=-256.0)
                kb.inc(i, ph + "msk", 1)
        mskmarks[h] = kb.val(ph + "msk")

        head_prep(h)

        @block.sync
        def _(q):
            kb.wait(q, ph + "act", actbases[h] + 1)
            i = q.dma_start(out=D["qspill"][h * 128:(h + 1) * 128, :], in_=qh[:])
            qsp_marks.append(kb.dinc(i, f"qsp{h % 2}"))

    for c in range(nch):
        scan_chunk(H - 1, c)

    ctx["marks"]["wring_fence"] = [(ph + "qk", kb.val(ph + "qk"))]
    ctx["marks"]["qspill" + str(L)] = list(qsp_marks)
    ctx["marks"][f"scan_dve{L}"] = (ph + "dve", kb.val(ph + "dve"))
    ctx["marks"][f"scan_pe{L}"] = (ph + "pe", kb.val(ph + "pe"))
    ctx["marks"][f"scan_act{L}"] = (ph + "act", kb.val(ph + "act"))

    @block.sync
    def _(q):
        kb.wait(q, ph + "dve")
        i = q.dma_start(out=D["st_in"][L], in_=S["Ssb"][:])
        ctx["marks"][f"st_dma{L}"] = kb.dinc(i, "std")

    @block.gpsimd
    def _(g):
        stm = ctx["marks"][f"st_dma{L}"]
        kb.wait(g, stm[0], stm[1])
        i = g.collective_compute("AllGather", ALU.bypass,
                                 replica_groups=[list(range(NCORES))],
                                 ins=[D["st_in"][L].opt()], outs=[D["st_out"][L].opt()])
        ctx["marks"][f"cc_st{L}"] = kb.inc(i, "ccq", 1)

    emit_tap(ctx, f"attn{L}_local", R3[:], (128, KC * T), BF16,
             [ctx["marks"][f"scan_dve{L}"]])
    emit_tap(ctx, f"send{L}", D["st_in"][L], (128, 2048), F32,
             [ctx["marks"][f"st_dma{L}"]])



def emit_gate(ctx):
    kb, block, D, S, PS = ctx["kb"], ctx["block"], ctx["D"], ctx["S"], ctx["PS"]
    ph = "g1"
    slots = [S["wring"][:, 0:2048], S["wring"][:, 2048:4096]]
    wfence = ctx["marks"]["wring_fence"]
    R1, R2 = S["R1"], S["R2"]
    B = {t: kb.val(ph + t) for t in ["mm", "ev"]}
    sact = ctx["marks"]["scan_act1"]
    spe = ctx["marks"]["scan_pe1"]

    @block.gpsimd
    def _(g):
        for tg, vv in wfence:
            kb.wait(g, tg, vv)
        for f in range(KC):
            if f >= 2:
                kb.wait(g, ph + "mm", B["mm"] + f - 1)
            i = g.dma_start(out=slots[f % 2][:], in_=D["wgate1_stream"][f])
            kb.dinc(i, f"g1w{f % 2}")
    g1w0 = [kb.val("g1w0") - 16 * 8, kb.val("g1w1") - 16 * 8]

    @block.tensor
    def _(t):
        kb.wait(t, sact[0], sact[1])
        for f in range(KC):
            kb.wait(t, f"g1w{f % 2}", g1w0[f % 2] + 16 * (f // 2 + 1))
            if f >= 2:
                kb.wait(t, ph + "ev", B["ev"] + f - 1)
            mmh = [None, None]
            for kc in range(KC):
                for hf in range(2):
                    bank = (hf if f % 2 == 0 else 2 + hf)
                    mmh[hf] = t.matmul(PS[f"mm{bank}"][:],
                                       slots[f % 2][:, kc * 128:(kc + 1) * 128],
                                       R1[:, kc * T + hf * 512: kc * T + hf * 512 + 512],
                                       start=(kc == 0), stop=(kc == KC - 1))
            kb.inc(mmh[1], ph + "mm", 1)

    @block.scalar
    def _(s):
        kb.wait(s, spe[0], spe[1])
        for f in range(KC):
            kb.wait(s, ph + "mm", B["mm"] + f + 1)
            for hf in range(2):
                bank = (hf if f % 2 == 0 else 2 + hf)
                i = s.activation(out=R2[:, f * T + hf * 512: f * T + hf * 512 + 512],
                                 in_=PS[f"mm{bank}"][:], func=AF.Sigmoid)
            kb.inc(i, ph + "ev", 1)
    ctx["marks"]["wring_fence"] = [(ph + "mm", kb.val(ph + "mm"))]
    ctx["marks"]["gate_ev"] = (ph + "ev", kb.val(ph + "ev"))
    emit_tap(ctx, "gate1", R2[:], (128, KC * T), BF16, [ctx["marks"]["gate_ev"]])



def emit_ffnA(ctx, L):
    kb, block, D, S, PS = ctx["kb"], ctx["block"], ctx["D"], ctx["S"], ctx["PS"]
    ph = "fA"
    B = {t: kb.val(ph + t) for t in ["mm", "sil", "hj", "sqa", "squ"]}
    R1 = S["R1"]
    wfence = ctx["marks"]["wring_fence"]
    slots = [S["wring"][:, 0:4096], S["wring"][:, 4096:8192]]
    hj = [S["hj0"], S["hj1"]]
    bank_fence = ctx["marks"]["scan_act0" if L == 0 else "gate_ev"]
    if L == 0:
        GB = [[PS["mm0"][:], PS["mm1"][:]], [PS["scanfull"], PS["b7full"]]]
        UB = [[PS["mm2"][:], PS["mm3"][:]], [PS["mm2"][:], PS["mm3"][:]]]
    else:
        GB = [[PS["mm0"][:], PS["mm1"][:]], [PS["ex0"], PS["ex1"]]]
        UB = [[PS["mm2"][:], PS["mm3"][:]], [PS["scanfull"], PS["b7full"]]]

    hw_base = [kb.val("hw0"), kb.val("hw1")]

    def h0_store(q, j):
        kb.wait(q, ph + "hj", B["hj"] + 2 * (j + 1))
        i = q.dma_start(out=D["h0"][0, j // 8, :, (j % 8) * 512:(j % 8) * 512 + 512],
                        in_=hj[j % 2][:, 0:512])
        kb.dinc(i, f"hw{j % 2}")
        i = q.dma_start(out=D["h0"][1, j // 8, :, (j % 8) * 512:(j % 8) * 512 + 512],
                        in_=hj[j % 2][:, 512:1024])
        kb.dinc(i, f"hw{j % 2}")

    if L == 0:
        @block.sync
        def _(q):
            kb.wait(q, "ccq", ctx["marks"]["cc_gu"])
            for tg, vv in wfence:
                kb.wait(q, tg, vv)
            for j in range(64):
                if j >= 2:
                    kb.wait(q, ph + "mm", B["mm"] + j - 1)
                i = q.dma_start(out=slots[j % 2][:], in_=D["gu_out"][j * 128:(j + 1) * 128, :])
                kb.dinc(i, f"fAs{j % 2}")
                if j >= 1:
                    h0_store(q, j - 1)
            h0_store(q, 63)
        fAw0 = [kb.val("fAs0") - 16 * 32, kb.val("fAs1") - 16 * 32]
        fAtag = "fAs"
    else:
        @block.gpsimd
        def _(g):
            for tg, vv in wfence:
                kb.wait(g, tg, vv)
            for j in range(64):
                if j >= 2:
                    kb.wait(g, ph + "mm", B["mm"] + j - 1)
                i = g.dma_start(out=slots[j % 2][:], in_=D["gu1_stream"][j])
                kb.dinc(i, f"fAw{j % 2}")
        fAw0 = [kb.val("fAw0") - 16 * 32, kb.val("fAw1") - 16 * 32]
        fAtag = "fAw"

    @block.tensor
    def _(t):
        kb.wait(t, bank_fence[0], bank_fence[1])
        kb.wait(t, "sdve", kb.val("sdve"))
        if L == 1:
            kb.wait(t, "fBv", kb.val("fBv"))
            kb.wait(t, "lnrs", kb.val("lnrs"))
        for j in range(64):
            kb.wait(t, fAtag + str(j % 2), fAw0[j % 2] + 16 * (j // 2 + 1))
            if j >= 2:
                kb.wait(t, ph + "sil", B["sil"] + j - 1)
            gb, ub = GB[j % 2], UB[j % 2]
            mm = [None] * 4
            for kc in range(KC):
                for hf in range(2):
                    mm[hf] = t.matmul(gb[hf],
                                      slots[j % 2][:, kc * 128:(kc + 1) * 128],
                                      R1[:, kc * T + hf * 512: kc * T + hf * 512 + 512],
                                      start=(kc == 0), stop=(kc == KC - 1))
            if L == 0 and j >= 1:
                kb.wait(t, ph + "hj", B["hj"] + 2 * j)
            for kc in range(KC):
                for hf in range(2):
                    mm[2 + hf] = t.matmul(ub[hf],
                                          slots[j % 2][:, 2048 + kc * 128: 2048 + (kc + 1) * 128],
                                          R1[:, kc * T + hf * 512: kc * T + hf * 512 + 512],
                                          start=(kc == 0), stop=(kc == KC - 1))
            kb.inc(mm[3], ph + "mm", 1)
            if L == 0:
                for jj_ in ([j - 1] if j >= 1 else []) + ([63] if j == 63 else []):
                    kb.wait(t, ph + "sqa", B["sqa"] + jj_ + 1)
                    if jj_ == 0:
                        kb.wait(t, "fBv", kb.val("fBv"))
                        kb.wait(t, "nzv", kb.val("nzv"))
                        kb.wait(t, "lnrs", kb.val("lnrs"))
                    for hf in range(2):
                        mms = t.matmul(PS[f"ex{hf}"][:], S["ones"][:],
                                       S["sq"][:, hf * 512:(hf + 1) * 512],
                                       start=(jj_ == 0), stop=(jj_ == 63))
                    kb.inc(mms, ph + "squ", 1)

    @block.scalar
    def _(s):
        for j in range(64):
            kb.wait(s, ph + "mm", B["mm"] + j + 1)
            gb, ub = GB[j % 2], UB[j % 2]
            if L == 0:
                s.activation(out=S["fscr"][:, 0:512], in_=gb[0], func=AF.Silu)
                i = s.activation(out=S["fscr"][:, 512:1024], in_=gb[1], func=AF.Silu)
                kb.inc(i, ph + "sil", 1)
                kb.wait(s, ph + "hj", B["hj"] + 2 * (j + 1))
                kb.wait(s, ph + "squ", B["squ"] + j)
                i = s.activation(out=S["sq"][:], in_=hj[j % 2][:], func=AF.Square)
                kb.inc(i, ph + "sqa", 1)
            else:
                s.activation(out=S["fscr"][:, 0:512], in_=gb[0], func=AF.Silu,
                             scale=S["sgate"][:, j:j + 1])
                s.activation(out=S["fscr"][:, 512:1024], in_=gb[1], func=AF.Silu,
                             scale=S["sgate"][:, j:j + 1])
                s.activation(out=S["fscr"][:, 1024:1536], in_=ub[0], func=AF.Identity,
                             scale=S["sup"][:, j:j + 1])
                i = s.activation(out=S["fscr"][:, 1536:2048], in_=ub[1], func=AF.Identity,
                                 scale=S["sup"][:, j:j + 1])
                kb.inc(i, ph + "sil", 1)

    @block.vector
    def _(v):
        for j in range(64):
            kb.wait(v, ph + "sil", B["sil"] + j + 1)
            if j >= 2:
                kb.wait(v, f"hw{j % 2}", hw_base[j % 2] + 32 * (j // 2))
                if L == 0:
                    kb.wait(v, ph + "sqa", B["sqa"] + j - 1)
            for hf in range(2):
                sl = slice(hf * 512, (hf + 1) * 512)
                if L == 0:
                    i = v.tensor_tensor(hj[j % 2][:, sl], UB[j % 2][hf],
                                        S["fscr"][:, sl], ALU.mult)
                else:
                    i = v.tensor_tensor(hj[j % 2][:, sl], S["fscr"][:, sl],
                                        S["fscr"][:, 1024 + hf * 512: 1024 + (hf + 1) * 512],
                                        ALU.mult)
                kb.inc(i, ph + "hj", 1)

    if L == 1:
        @block.sync
        def _(q):
            for j in range(64):
                h0_store(q, j)
    ctx["marks"][f"h0mark{L}"] = [("hw0", kb.val("hw0")), ("hw1", kb.val("hw1"))]
    if L == 0:
        emit_tap(ctx, "h0tap", D["h0"].rearrange("a b p c -> (a b p) c"), (2 * 8 * 128, 4096), BF16,
                 ctx["marks"][f"h0mark{L}"])
    ctx["marks"]["wring_fence"] = [(ph + "mm", kb.val(ph + "mm"))]
    ctx["marks"]["fscr_fence"] = (ph + "hj", kb.val(ph + "hj"))
    ctx["marks"]["R1_fence"] = (ph + "mm", kb.val(ph + "mm"))
    ctx["marks"][f"fA_mm{L}"] = (ph + "mm", kb.val(ph + "mm"))
    ctx["marks"]["fA_sil"] = (ph + "sil", kb.val(ph + "sil"))
    ctx["marks"]["fA_hj"] = (ph + "hj", kb.val(ph + "hj"))
    if L == 0:
        ctx["marks"]["fAst"] = (ph + "squ", B["squ"] + 64)



def emit_corr(ctx, L):
    kb, block, D, S, PS = ctx["kb"], ctx["block"], ctx["D"], ctx["S"], ctx["PS"]
    ph = "cr"
    B = {t: kb.val(ph + t) for t in ["sel", "pre", "dve", "act", "pe"]}
    R3 = S["R3"]
    fence = ctx["marks"]["g_done0"] if L == 0 else ctx["marks"]["fscr_fence"]
    fAsil = ctx["marks"]["fA_sil"]
    sel_marks = []
    dve_base = kb.val(ph + "dve")

    @block.scalar
    def _(q):
        kb.wait(q, "ccq", ctx["marks"][f"cc_st{L}"])
        kb.wait(q, fence[0], fence[1])
        if L == 1:
            kb.wait(q, "crpe", kb.val("crpe"))
        for s_ in range(8):
            if s_ >= 2:
                kb.wait(q, ph + "sel", B["sel"] + s_ - 1)
            i = q.dma_start(out=S["fscr"][:, (s_ % 2) * 2048:(s_ % 2) * 2048 + 2048],
                            in_=D["st_out"][L][s_ * 128:(s_ + 1) * 128, :])
            sel_marks.append(kb.dinc(i, f"sel{s_ % 2}"))

    @block.vector
    def _(v):
        for s_ in range(8):
            kb.wait(v, sel_marks[s_][0], sel_marks[s_][1])
            slot = S["fscr"][:, (s_ % 2) * 2048:(s_ % 2) * 2048 + 2048]
            if s_ == 0:
                i = v.tensor_scalar(S["Sprev"][:], slot, S["onehot"][:, 0:1], None, ALU.mult)
            else:
                i = v.scalar_tensor_tensor(S["Sprev"][:], slot, S["onehot"][:, s_:s_ + 1],
                                           S["Sprev"][:], ALU.mult, ALU.add)
            kb.inc(i, ph + "sel", 1)
    emit_tap(ctx, f"sprev{L}", S["Sprev"][:], (128, 2048), F32, [(ph + "sel", B["sel"] + 8)])

    qsp = ctx["marks"]["qspill" + str(L)]
    dve0 = kb.val(ph + "dve")
    DPH = 3
    for h in range(H):
        scol = S["scols"][:, h:h + 1]
        par = h % 2
        qslot = S["qh"] if par == 0 else S["kh"]
        lam_r = S["fscr"][:, par * 1024: par * 1024 + 1024]
        qp_r = S["fscr"][:, 2048 + par * 1024: 2048 + par * 1024 + 1024]

        @block.sync
        def _(q):
            kb.wait(q, qsp[h][0], qsp[h][1])
            if h <= 1:
                kb.wait(q, ph + "sel", B["sel"] + 8)
            else:
                kb.wait(q, ph + "dve", dve0 + DPH * (h - 1))
            i = q.dma_start(out=qslot[:], in_=D["qspill"][h * 128:(h + 1) * 128, :])
            qmark = kb.dinc(i, f"crq{par}")
        qmark = (f"crq{par}", kb.val(f"crq{par}"))

        if L == 1:
            @block.vector
            def _(v):
                if h >= 1:
                    kb.wait(v, ph + "act", B["act"] + h)
                i = v.tensor_scalar(S["rstd"][:], S["tpos1"][:], scol, None, ALU.mult)
                kb.inc(i, ph + "pre", 1)

        @block.scalar
        def _(s):
            if h <= 1:
                kb.wait(s, ph + "sel", B["sel"] + 8)
                kb.wait(s, fAsil[0], fAsil[1])
            else:
                kb.wait(s, ph + "dve", dve0 + DPH * (h - 1))
            if L == 0:
                i = s.activation(out=lam_r, in_=S["tpos1"][:], func=AF.Exp,
                                 scale=-L0_SLOPES[h])
            else:
                kb.wait(s, ph + "pre", B["pre"] + h + 1)
                i = s.activation(out=lam_r, in_=S["rstd"][:], func=AF.Exp, scale=-1.0)
            kb.inc(i, ph + "act", 1)
        amark = kb.val(ph + "act")

        @block.vector
        def _(v):
            kb.wait(v, ph + "act", amark)
            kb.wait(v, qmark[0], qmark[1])
            i = v.tensor_tensor(qp_r, qslot[:], lam_r, ALU.mult)
            kb.inc(i, ph + "dve", 1)
        dmark = kb.val(ph + "dve")

        @block.tensor
        def _(t):
            kb.wait(t, ph + "dve", dmark)
            i = t.matmul(PS["mm0" if par == 0 else "mm2"][:],
                         S["Sprev"][:, h * 128:(h + 1) * 128],
                         qp_r[:, 0:512], start=True, stop=True)
            i = t.matmul(PS["mm1" if par == 0 else "mm3"][:],
                         S["Sprev"][:, h * 128:(h + 1) * 128],
                         qp_r[:, 512:1024], start=True, stop=True)
            kb.inc(i, ph + "pe", 1)
        pmark = kb.val(ph + "pe")

        @block.vector
        def _(v):
            kb.wait(v, ph + "pe", pmark)
            v.tensor_tensor(R3[:, h * T: h * T + 512], R3[:, h * T: h * T + 512],
                            PS["mm0" if par == 0 else "mm2"][:], ALU.add)
            i = v.tensor_tensor(R3[:, h * T + 512: h * T + 1024],
                                R3[:, h * T + 512: h * T + 1024],
                                PS["mm3" if par == 1 else "mm1"][:],
                                ALU.add)
            kb.inc(i, ph + "dve", 2)

    ctx["marks"]["fscr_fence"] = (ph + "dve", kb.val(ph + "dve"))
    ctx["marks"][f"corr_done{L}"] = (ph + "dve", kb.val(ph + "dve"))
    emit_tap(ctx, f"attn{L}", R3[:], (128, KC * T), BF16,
             [ctx["marks"][f"corr_done{L}"]])



def emit_normz(ctx):
    kb, block, D, S, PS = ctx["kb"], ctx["block"], ctx["D"], ctx["S"], ctx["PS"]
    ph = "nz"
    R1, R2, R3 = S["R1"], S["R2"], S["R3"]
    cd = ctx["marks"]["corr_done1"]
    B = {t: kb.val(ph + t) for t in ["squ", "sqa", "v", "rs", "t1", "t2", "z"]}

    @block.scalar
    def _(s):
        kb.wait(s, cd[0], cd[1])
        for f in range(KC):
            kb.wait(s, ph + "squ", B["squ"] + f)
            i = s.activation(out=S["sq"][:], in_=R3[:, f * T:(f + 1) * T], func=AF.Square)
            kb.inc(i, ph + "sqa", 1)

    @block.tensor
    def _(t):
        kb.wait(t, "fBv", kb.val("fBv"))
        kb.wait(t, "lnrs", kb.val("lnrs"))
        kb.wait(t, "fAsil", kb.val("fAsil"))
        for f in range(KC):
            kb.wait(t, ph + "sqa", B["sqa"] + f + 1)
            for hf in range(2):
                mm = t.matmul(PS[f"ex{hf}"][:], S["ones"][:],
                              S["sq"][:, hf * 512:(hf + 1) * 512],
                              start=(f == 0), stop=(f == KC - 1))
            kb.inc(mm, ph + "squ", 1)

    @block.scalar
    def _(s):
        kb.wait(s, ph + "squ", B["squ"] + KC)
        for hf in range(2):
            i = s.activation(out=S["fscr"][:, 3072 + hf * 512: 3072 + (hf + 1) * 512],
                             in_=PS[f"ex{hf}"][:], func=AF.Sqrt, scale=1.0 / 2048,
                             bias=S['epsv'][:, 1:2])
            kb.inc(i, ph + "v", 1)

    @block.vector
    def _(v):
        for hf in range(2):
            kb.wait(v, ph + "v", B["v"] + hf + 1)
            i = v.reciprocal(S["rstdf"][:, hf * 512:(hf + 1) * 512],
                             S["fscr"][:, 3072 + hf * 512: 3072 + (hf + 1) * 512])
            kb.inc(i, ph + "rs", 1)

    r1f = ctx["marks"]["R1_fence"]
    for f in range(KC):
        @block.vector
        def _(v, f=f):
            if f == 0:
                kb.wait(v, ph + "rs", B["rs"] + 2)
            kb.wait(v, ph + "t2", B["t2"] + f)
            i = v.tensor_tensor(S["fscr"][:, 0:1024], R3[:, f * T:(f + 1) * T],
                                S["rstdf"][:], ALU.mult)
            kb.inc(i, ph + "t1", 1)

        @block.scalar
        def _(s, f=f):
            kb.wait(s, ph + "t1", B["t1"] + f + 1)
            kb.wait(s, ph + "z", B["z"] + f)
            i = s.activation(out=S["fscr"][:, 1024:2048], in_=S["fscr"][:, 0:1024],
                             func=AF.Identity, scale=S["normw"][:, f:f + 1])
            kb.inc(i, ph + "t2", 1)

        @block.vector
        def _(v, f=f):
            if f == 0:
                kb.wait(v, r1f[0], r1f[1])
            kb.wait(v, ph + "t2", B["t2"] + f + 1)
            i = v.tensor_tensor(R1[:, f * T:(f + 1) * T], S["fscr"][:, 1024:2048],
                                R2[:, f * T:(f + 1) * T], ALU.mult)
            kb.inc(i, ph + "z", 1)
    ctx["marks"]["fscr_fence"] = (ph + "z", kb.val(ph + "z"))
    ctx["marks"]["normz_done"] = (ph + "z", kb.val(ph + "z"))
    emit_tap(ctx, "z1", R1[:], (128, KC * T), BF16, [ctx["marks"]["normz_done"]])



def emit_ffnB(ctx, L):
    kb, block, D, S, PS = ctx["kb"], ctx["block"], ctx["D"], ctx["S"], ctx["PS"]
    ph = "fB"
    B = {t: kb.val(ph + t) for t in ["mm", "v", "rs", "gt", "gact"]}
    R2 = S["R2"]
    wfence = ctx["marks"]["wring_fence"]
    wslots = [S["wring"][:, 0:4096], S["wring"][:, 4096:8192]]
    hslots = [S["qscr"][:, 0:4096], S["qscr"][:, 4096:8192]]
    fa_sil = ctx["marks"]["fA_sil"]
    fa_hj = ctx["marks"]["fA_hj"]
    bank_fence = [fa_sil, fa_hj] if L == 0 else [ctx["marks"]["corr_done1"]]

    if L == 0:
        fam = ctx["marks"]["fAst"]

        @block.scalar
        def _(s):
            kb.wait(s, fam[0], fam[1])
            for hf in range(2):
                i = s.activation(out=S["fscr"][:, 3072 + hf * 512: 3072 + (hf + 1) * 512],
                                 in_=PS[f"ex{hf}"][:], func=AF.Sqrt, scale=1.0 / 8192,
                                 bias=S['epsv'][:, 2:3])
                kb.inc(i, ph + "v", 1)

        @block.vector
        def _(v):
            for hf in range(2):
                kb.wait(v, ph + "v", B["v"] + hf + 1)
                i = v.reciprocal(S["rstdf"][:, hf * 512:(hf + 1) * 512],
                                 S["fscr"][:, 3072 + hf * 512: 3072 + (hf + 1) * 512])
                kb.inc(i, ph + "rs", 1)

    wmarks = []
    hmarks = []
    if L == 0:
        @block.gpsimd
        def _(q):
            rtl = ctx["marks"]["retile"]
            kb.wait(q, rtl[0], rtl[1])
            for tg, vv in wfence:
                kb.wait(q, tg, vv)
            n = 0
            for mg in range(4):
                for hf in range(2):
                    for blk in range(8):
                        if n >= 2:
                            kb.wait(q, ph + "mm", B["mm"] + n - 1)
                        i = q.dma_start(out=wslots[n % 2][:], in_=D["wpj_lin"][mg, blk])
                        wmarks.append(kb.dinc(i, f"fBw{n % 2}"))
                        n += 1
    else:
        @block.gpsimd
        def _(g):
            for tg, vv in wfence:
                kb.wait(g, tg, vv)
            n = 0
            for mg in range(4):
                for hf in range(2):
                    for blk in range(8):
                        if n >= 2:
                            kb.wait(g, ph + "mm", B["mm"] + n - 1)
                        i = g.dma_start(out=wslots[n % 2][:], in_=D["wdown_lin"][mg, blk])
                        wmarks.append(kb.dinc(i, f"fBw{n % 2}"))
                        n += 1

    @block.sync
    def _(q):
        for tg, vv in ctx["marks"][f"h0mark{L}"]:
            kb.wait(q, tg, vv)
        if L == 0:
            for tg, vv in ctx["marks"]["quant_store"]:
                kb.wait(q, tg, vv)
        n = 0
        for mg in range(4):
            for hf in range(2):
                for blk in range(8):
                    if n >= 2:
                        kb.wait(q, ph + "mm", B["mm"] + n - 1)
                    i = q.dma_start(out=hslots[n % 2][:], in_=D["h0"][hf, blk])
                    hmarks.append(kb.dinc(i, f"fBh{n % 2}"))
                    n += 1

    SEGB = [[PS["mm0"][:], PS["mm1"][:], PS["mm2"][:], PS["mm3"][:]],
            [PS["ex0"], PS["ex1"], PS["scanfull"], PS["b7full"]]]

    @block.tensor
    def _(t):
        for tg, vv in bank_fence:
            kb.wait(t, tg, vv)
        kb.wait(t, "sdve", kb.val("sdve"))
        if L == 1:
            kb.wait(t, "nzv", kb.val("nzv"))
        n = 0
        for mg in range(4):
            for hf in range(2):
                seg = mg * 2 + hf
                if seg == 1 and L == 0:
                    kb.wait(t, ph + "v", B["v"] + 2)
                if seg >= 2:
                    tag = ph + ("gt" if L == 0 else "gact")
                    base = B["gt" if L == 0 else "gact"]
                    kb.wait(t, tag, base + 4 * (seg - 1))
                sb_ = SEGB[seg % 2]
                for blk in range(8):
                    kb.wait(t, wmarks[n][0], wmarks[n][1])
                    kb.wait(t, hmarks[n][0], hmarks[n][1])
                    mm = None
                    for jj in range(8):
                        for mt in range(4):
                            mm = t.matmul(sb_[mt],
                                          wslots[n % 2][:, jj * 512 + mt * 128: jj * 512 + (mt + 1) * 128],
                                          hslots[n % 2][:, jj * 512:(jj + 1) * 512],
                                          start=(blk == 0 and jj == 0),
                                          stop=(blk == 7 and jj == 7))
                    kb.inc(mm, ph + "mm", 1)
                    n += 1

    if L == 0:
        spe = ctx["marks"]["scan_pe0"]

        @block.vector
        def _(v):
            kb.wait(v, ph + "rs", B["rs"] + 2)
            for seg in range(8):
                mg, hf = seg // 2, seg % 2
                kb.wait(v, ph + "mm", B["mm"] + 8 * (seg + 1))
                for mt in range(4):
                    n_ = seg * 4 + mt
                    kb.wait(v, ph + "gact", B["gact"] + max(0, n_ - 1))
                    i = v.tensor_tensor(S["fscr"][:, 2048 + (mt % 2) * 512: 2048 + (mt % 2) * 512 + 512],
                                        SEGB[seg % 2][mt], S["rstdf"][:, hf * 512:(hf + 1) * 512],
                                        ALU.mult)
                    kb.inc(i, ph + "gt", 1)

        @block.scalar
        def _(s):
            kb.wait(s, spe[0], spe[1])
            for seg in range(8):
                mg, hf = seg // 2, seg % 2
                for mt in range(4):
                    kb.wait(s, ph + "gt", B["gt"] + seg * 4 + mt + 1)
                    m = mg * 4 + mt
                    i = s.activation(out=R2[:, m * T + hf * 512: m * T + hf * 512 + 512],
                                     in_=S["fscr"][:, 2048 + (mt % 2) * 512: 2048 + (mt % 2) * 512 + 512],
                                     func=AF.Silu)
                    kb.inc(i, ph + "gact", 1)
    else:
        nz = ctx["marks"]["normz_done"]

        @block.scalar
        def _(s):
            kb.wait(s, nz[0], nz[1])
            for seg in range(8):
                mg, hf = seg // 2, seg % 2
                kb.wait(s, ph + "mm", B["mm"] + 8 * (seg + 1))
                for mt in range(4):
                    m = mg * 4 + mt
                    i = s.activation(out=R2[:, m * T + hf * 512: m * T + hf * 512 + 512],
                                     in_=SEGB[seg % 2][mt], func=AF.Silu,
                                     scale=S["sdown"][:, m:m + 1])
                    kb.inc(i, ph + "gact", 1)

    ctx["marks"]["wring_fence"] = [(ph + "mm", kb.val(ph + "mm"))]
    ctx["marks"][f"g_done{L}"] = (ph + "gact", kb.val(ph + "gact"))
    emit_tap(ctx, f"g{L}", R2[:], (128, KC * T), BF16, [ctx["marks"][f"g_done{L}"]])



def emit_out(ctx, L):
    kb, block, D, S, PS = ctx["kb"], ctx["block"], ctx["D"], ctx["S"], ctx["PS"]
    ph = "ou"
    B = {t: kb.val(ph + t) for t in ["mm", "sact", "mdone", "dve"]}
    rhs = S["R3"] if L == 0 else S["R1"]
    g = S["R2"]
    stream = D["wout0_stream"] if L == 0 else D["wout1_stream"]
    if L == 0:
        x_src = D["xT"]
        x_dst = D["x_spill"]
    else:
        x_src = D["x_spill"] if 0 in ctx["layers"] else D["x1T"]
        x_dst = D["out"]
    slots = [S["wring"][:, 0:2048], S["wring"][:, 2048:4096]]
    wfence = ctx["marks"]["wring_fence"]
    gdone = ctx["marks"][f"g_done{L}"]
    crd = ctx["marks"][f"corr_done{L}"]
    xres = [S["fscr"][:, 0:1024], S["fscr"][:, 1024:2048]]
    xo = [S["fscr"][:, 2048:3072], S["fscr"][:, 3072:4096]]
    x1_ready = ctx["marks"].get("x1mark") if L == 1 else None

    @block.gpsimd
    def _(gq):
        for tg, vv in wfence:
            kb.wait(gq, tg, vv)
        for f in range(KC):
            if f >= 2:
                kb.wait(gq, ph + "mm", B["mm"] + f - 1)
            i = gq.dma_start(out=slots[f % 2][:], in_=stream[f])
            kb.dinc(i, f"ouw{f % 2}")
    ouw0 = [kb.val("ouw0") - 16 * 8, kb.val("ouw1") - 16 * 8]

    xr_marks = []
    xw_marks = []

    @block.sync
    def _(q):
        kb.wait(q, gdone[0], gdone[1])
        kb.wait(q, crd[0], crd[1])
        if x1_ready:
            for tg, vv in x1_ready:
                kb.wait(q, tg, vv)
        for f in range(KC):
            if f >= 2:
                kb.wait(q, ph + "dve", B["dve"] + 2 * (f - 1))
            i = q.dma_start(out=xres[f % 2][:], in_=x_src[f * 128:(f + 1) * 128, :])
            xr_marks.append(kb.dinc(i, f"oux{f % 2}"))
            if f >= 1:
                fo = f - 1
                kb.wait(q, ph + "dve", B["dve"] + 2 * (fo + 1))
                i = q.dma_start(out=x_dst[fo * 128:(fo + 1) * 128, :], in_=xo[fo % 2][:])
                xw_marks.append(kb.dinc(i, f"ous{fo % 2}"))
        fo = KC - 1
        kb.wait(q, ph + "dve", B["dve"] + 2 * (fo + 1))
        i = q.dma_start(out=x_dst[fo * 128:(fo + 1) * 128, :], in_=xo[fo % 2][:])
        xw_marks.append(kb.dinc(i, f"ous{fo % 2}"))

    @block.tensor
    def _(t):
        kb.wait(t, gdone[0], gdone[1])
        kb.wait(t, crd[0], crd[1])
        for f in range(KC):
            kb.wait(t, f"ouw{f % 2}", ouw0[f % 2] + 16 * (f // 2 + 1))
            b0 = 0 if f % 2 == 0 else 2
            if f >= 2:
                if L == 0:
                    kb.wait(t, ph + "sact", B["sact"] + 2 * (f - 1))
                else:
                    kb.wait(t, ph + "dve", B["dve"] + 2 * (f - 1))
            mmh = [None, None]
            for kc in range(KC):
                for hf in range(2):
                    mmh[hf] = t.matmul(PS[f"mm{b0 + hf}"][:],
                                       slots[f % 2][:, kc * 128:(kc + 1) * 128],
                                       rhs[:, kc * T + hf * 512: kc * T + hf * 512 + 512],
                                       start=(kc == 0), stop=(kc == KC - 1))
            kb.inc(mmh[1], ph + "mm", 1)

    if L == 0:
        @block.scalar
        def _(s):
            for f in range(KC):
                kb.wait(s, ph + "mm", B["mm"] + f + 1)
                b0 = 0 if f % 2 == 0 else 2
                for hf in range(2):
                    kb.wait(s, ph + "mdone", B["mdone"] + max(0, 2 * f + hf - 1))
                    i = s.activation(out=S["mean"][:, hf * 512:(hf + 1) * 512],
                                     in_=PS[f"mm{b0 + hf}"][:], func=AF.Identity,
                                     bias=S["bo"][:, f:f + 1])
                    kb.inc(i, ph + "sact", 1)

    @block.vector
    def _(v):
        for f in range(KC):
            b0 = 0 if f % 2 == 0 else 2
            kb.wait(v, xr_marks[f][0], xr_marks[f][1])
            if f >= 2:
                kb.wait(v, xw_marks[f - 2][0], xw_marks[f - 2][1])
            for hf in range(2):
                sl = slice(hf * 512, (hf + 1) * 512)
                if L == 0:
                    kb.wait(v, ph + "sact", B["sact"] + 2 * f + hf + 1)
                    i = v.tensor_tensor(S["rstd"][:, sl], S["mean"][:, sl],
                                        g[:, f * T + hf * 512: f * T + hf * 512 + 512],
                                        ALU.mult)
                    kb.inc(i, ph + "mdone", 1)
                else:
                    kb.wait(v, ph + "mm", B["mm"] + f + 1)
                    i = v.tensor_tensor(S["rstd"][:, sl], PS[f"mm{b0 + hf}"][:],
                                        g[:, f * T + hf * 512: f * T + hf * 512 + 512],
                                        ALU.mult)
                i = v.tensor_tensor(xo[f % 2][:, sl], S["rstd"][:, sl], xres[f % 2][:, sl],
                                    ALU.add)
                kb.inc(i, ph + "dve", 1)

    if L == 0:
        ctx["marks"]["x1mark"] = [("ous0", kb.val("ous0")), ("ous1", kb.val("ous1"))]
    else:
        ctx["marks"]["outmark"] = [("ous0", kb.val("ous0")), ("ous1", kb.val("ous1"))]
    ctx["marks"]["wring_fence"] = [(ph + "mm", kb.val(ph + "mm"))]
    ctx["marks"]["fscr_fence"] = (ph + "dve", kb.val(ph + "dve"))
    ctx["marks"]["R2_fence"] = (ph + "dve", kb.val(ph + "dve"))
    ctx["marks"]["ousact"] = kb.val(ph + "sact")
    ctx["marks"]["oudve"] = kb.val(ph + "dve")
    emit_tap(ctx, f"x{L + 1}", x_dst, (2048, 1024), F32,
             [("ous0", kb.val("ous0")), ("ous1", kb.val("ous1"))])



def kernel(**inputs):
    maps = prep_inputs(inputs)
    nc, ctx = build_program(layers=(0, 1))
    res = run_bass_kernel_spmd(nc, maps, core_ids=list(range(NCORES)))
    out = np.zeros((4, 2048, 2048), dtype=np.float32)
    for c in range(NCORES):
        b, half = c // 2, c % 2
        out[b, half * 1024:(half + 1) * 1024, :] = np.asarray(res.results[c]["out"]).T
    return out
